# version 02
# baseline (speedup 1.0000x reference)
import numpy as np
import ml_dtypes
import concourse.bass as bass
import concourse.mybir as mybir
from concourse.bass_utils import run_bass_kernel_spmd

F32 = mybir.dt.float32
BF16 = mybir.dt.bfloat16
BF = ml_dtypes.bfloat16
AF = mybir.ActivationFunctionType
ALU = mybir.AluOpType
AX = mybir.AxisListType.X

ENGINES = ("tensor", "vector", "scalar", "gpsimd", "sync")


class Prog:

    def __init__(self, nc):
        self.nc = nc
        self.ops = []
        self.sem_names = []
        self.counts = {}

    def sem(self, name):
        if name not in self.counts:
            self.counts[name] = 0
            self.sem_names.append(name)
        return name

    def val(self, name):
        return self.counts.get(name, 0)

    def emit(self, engine, fn, waits=(), inc=None, amt=None, tag=""):
        w = [(s, v) for (s, v) in waits]
        if inc is not None:
            self.sem(inc)
            if amt is None:
                amt = 16 if fn.__name__ == "dma" or "dma" in tag else 1
            self.counts[inc] += amt
        self.ops.append(dict(engine=engine, fn=fn, waits=w, inc=inc, amt=amt, tag=tag))
        return self.counts[inc] if inc is not None else None

    def dma(self, engine, out, in_, waits=(), inc=None, tag="dma", **kw):
        def fn(eng, out=out, in_=in_, kw=kw):
            return eng.dma_start(out, in_, **kw)
        fn.__name__ = "dma"
        return self.emit(engine, fn, waits, inc, amt=16, tag=tag)

    def dmat(self, engine, out, in_, waits=(), inc=None, tag="dmat"):
        def fn(eng, out=out, in_=in_):
            return eng.dma_start(out, in_, transpose=True)
        fn.__name__ = "dma"
        return self.emit(engine, fn, waits, inc, amt=16, tag=tag)

    def build(self, block_sems):
        per_engine = {e: [] for e in ENGINES}
        for op in self.ops:
            per_engine[op["engine"]].append(op)
        known = {e: {} for e in ENGINES}

        def emit_for(eng_name, eng):
            for op in per_engine[eng_name]:
                for (s, v) in op["waits"]:
                    if v <= 0:
                        continue
                    if known[eng_name].get(s, 0) >= v:
                        continue
                    eng.wait_ge(block_sems[s], v)
                    known[eng_name][s] = v
                inst = op["fn"](eng)
                if op["inc"] is not None:
                    inst.then_inc(block_sems[op["inc"]], op["amt"])
        return emit_for


def run_prog(nc, prog, final_waits):
    import contextlib
    with contextlib.ExitStack() as st:
        sems = {}
        for name in prog.sem_names:
            sems[name] = st.enter_context(nc.semaphore(name))
        block = st.enter_context(nc.Block())
        emit_for = prog.build(sems)

        @block.tensor
        def _(t):
            emit_for("tensor", t)

        @block.vector
        def _(v):
            emit_for("vector", v)

        @block.scalar
        def _(s):
            emit_for("scalar", s)

        @block.gpsimd
        def _(g):
            emit_for("gpsimd", g)

        @block.sync
        def _(s):
            emit_for("sync", s)
            for (name, v) in final_waits:
                s.wait_ge(sems[name], v)
    return nc



def build_l1(only_proj=None, skip_beta=False):
    nc = bass.Bass("TRN2", target_bir_lowering=False)
    X = nc.dram_tensor("x", [1024, 2048], F32, kind="ExternalInput").ap()
    XHL = nc.dram_tensor("xhl", [16, 2048], F32, kind="ExternalInput").ap()
    CW = {n: nc.dram_tensor(f"cw{n}", [128, 16 * 4], F32, kind="ExternalInput").ap()
          for n in ("q", "k", "v")}
    CB = {n: nc.dram_tensor(f"cb{n}", [128, 16], F32, kind="ExternalInput").ap()
          for n in ("q", "k", "v")}
    WT = {}
    for n in ("wq", "wk", "wv", "wa", "wg"):
        WT[n] = nc.dram_tensor(n, [2048, 2048], F32, kind="ExternalInput").ap()
    WBT = nc.dram_tensor("wb", [2048, 16], F32, kind="ExternalInput").ap()
    BA = nc.dram_tensor("ba", [128, 16], F32, kind="ExternalInput").ap()
    BB = nc.dram_tensor("bb", [16, 1], F32, kind="ExternalInput").ap()
    OUT = {}
    for n, dt in (("qr", BF16), ("kr", BF16), ("vr", BF16), ("gg", BF16)):
        OUT[n] = nc.dram_tensor(n, [2048, 1024], dt, kind="ExternalOutput").ap()
    OUT["zz"] = nc.dram_tensor("zz", [2048, 1024], F32, kind="ExternalOutput").ap()
    OUT["bl"] = nc.dram_tensor("bl", [16, 1024], F32, kind="ExternalOutput").ap()

    p = Prog(nc)
    import contextlib
    stack = contextlib.ExitStack()
    sb = lambda name, shape, dt: stack.enter_context(nc.sbuf_tensor(name, shape, dt))
    ps = lambda name, shape: stack.enter_context(nc.psum_tensor(name, shape, F32))

    XT = [sb(f"xt{i}", [128, 2048], BF16) for i in range(2)]
    XTH = sb("xth", [16, 2048], BF16)
    XHALO = sb("xhalo", [128, 16 * 16], BF16)
    CWs = {n: sb(f"cws{n}", [128, 16 * 4], F32) for n in ("q", "k", "v")}
    CBs = {n: sb(f"cbs{n}", [128, 16], F32) for n in ("q", "k", "v")}
    YW = sb("yw", [128, 516], BF16)
    CVA = sb("cva", [128, 512], F32)
    SQ = sb("sq", [128, 2048], BF16)
    ST = sb("st", [128, 8], F32)
    XH = [sb(f"xh{i}", [128, 2048], BF16) for i in range(2)]
    XHFM = sb("xhfm", [128, 16 * 1024], BF16)
    WB = [sb(f"wb{i}", [128, 16 * 1024], BF16) for i in range(3)]
    WBTs = sb("wbts", [128, 16 * 16], BF16)
    BAs = sb("bas", [128, 16], F32)
    BBs = sb("bbs", [16, 1], F32)
    OB = [sb(f"ob{i}", [128, 512], F32) for i in range(4)]
    OBH = [sb(f"obh{i}", [128, 512], BF16) for i in range(4)]
    PB = [ps(f"pb{i}", [128, 512]) for i in range(4)]
    PBB = ps("pbb", [16, 512])

    for n in ("q", "k", "v"):
        p.dma("gpsimd", CWs[n][:], CW[n][:], inc="wload")
        p.dma("gpsimd", CBs[n][:], CB[n][:], inc="wload")
    hx = p.dma("gpsimd", XTH[:], XHL[:], inc="xldh")
    def hstat(e):
        e.reduce_sum(ST[0:16, 0:1], XTH[:], axis=AX)
        return e.engine_nop()
    p.emit("vector", hstat, waits=[("xldh", hx)], inc="hst")
    hsq = p.emit("scalar", lambda e: e.activation(SQ[0:16, :], XTH[:], AF.Square),
                 waits=[("xldh", hx)], inc="hst")
    def hstat2(e):
        e.reduce_sum(ST[0:16, 1:2], SQ[0:16, :], axis=AX)
        e.drain()
        e.tensor_scalar(ST[0:16, 2:3], ST[0:16, 0:1], 1.0 / 2048, None, ALU.mult)
        e.tensor_scalar(ST[0:16, 3:4], ST[0:16, 1:2], 1.0 / 2048, None, ALU.mult)
        e.drain()
        e.tensor_tensor(ST[0:16, 4:5], ST[0:16, 2:3], ST[0:16, 2:3], ALU.mult)
        e.drain()
        e.tensor_tensor(ST[0:16, 5:6], ST[0:16, 3:4], ST[0:16, 4:5], ALU.subtract)
        e.drain()
        e.tensor_scalar(ST[0:16, 5:6], ST[0:16, 5:6], 1e-5, None, ALU.add)
        e.drain()
        return e.reciprocal(ST[0:16, 6:7], ST[0:16, 5:6])
    hv2 = p.emit("vector", hstat2, waits=[("hst", 2)], inc="hst")
    hrs = p.emit("scalar", lambda e: e.activation(ST[0:16, 7:8], ST[0:16, 6:7], AF.Sqrt),
                 waits=[("hst", 3)], inc="hst")
    def hxh(e):
        e.drain()
        return e.tensor_scalar(XTH[:], XTH[:], ST[0:16, 2:3], ST[0:16, 7:8],
                               ALU.subtract, ALU.mult)
    hxv = p.emit("vector", hxh, waits=[("hst", 4)], inc="hst")
    p.dmat("sync", XHALO[:].rearrange("dp (dc t) -> dp dc t", dc=16), XTH[:],
           waits=[("hst", 5)], inc="xfmh")

    p.dma("gpsimd", WBTs[:].rearrange("dp (dc h) -> dp dc h", dc=16),
          WBT.rearrange("(dc dp) h -> dp dc h", dp=128), inc="wload")
    p.dma("gpsimd", BAs[:], BA[:], inc="wload")
    p.dma("gpsimd", BBs[:], BB[:], inc="wload")
    for i in range(8):
        b = i % 2
        xv = p.dma("gpsimd", XT[b][:], X[i * 128:(i + 1) * 128, :], inc="xld",
                   waits=[("xh", max(0, i - 1))])
        p.emit("vector", lambda e, b=b: e.reduce_sum(ST[:, 0:1], XT[b][:], axis=AX),
               waits=[("xld", xv)], inc="vst")
        sqv = p.emit("scalar", lambda e, b=b: e.activation(SQ[:], XT[b][:], AF.Square),
                     waits=[("xld", xv), ("sqr", i)], inc="sq")
        p.emit("vector", lambda e: e.reduce_sum(ST[:, 1:2], SQ[:], axis=AX),
               waits=[("sq", sqv)], inc="sqr")

        def stats_fn(e):
            e.drain()
            e.tensor_scalar(ST[:, 2:3], ST[:, 0:1], 1.0 / 2048, None, ALU.mult)
            e.tensor_scalar(ST[:, 3:4], ST[:, 1:2], 1.0 / 2048, None, ALU.mult)
            e.drain()
            e.tensor_tensor(ST[:, 4:5], ST[:, 2:3], ST[:, 2:3], ALU.mult)
            e.drain()
            e.tensor_tensor(ST[:, 5:6], ST[:, 3:4], ST[:, 4:5], ALU.subtract)
            e.drain()
            e.tensor_scalar(ST[:, 5:6], ST[:, 5:6], 1e-5, None, ALU.add)
            e.drain()
            return e.reciprocal(ST[:, 6:7], ST[:, 5:6])
        vv = p.emit("vector", stats_fn, inc="vst")
        rv = p.emit("scalar", lambda e: e.activation(ST[:, 7:8], ST[:, 6:7], AF.Sqrt),
                    waits=[("vst", vv), ("xh", i)], inc="rsq")
        def xh_fn(e, b=b):
            e.drain()
            return e.tensor_scalar(XH[b][:], XT[b][:], ST[:, 2:3], ST[:, 7:8],
                                   ALU.subtract, ALU.mult)
        hv = p.emit("vector", xh_fn,
                    waits=[("rsq", rv), ("xfm", max(0, i - 1) * 16)], inc="xh")
        out_ap = XHFM[:].rearrange("dp (dc t) -> dp dc t", dc=16)[:, :, i * 128:(i + 1) * 128]
        p.dmat("sync", out_ap, XH[b][:], waits=[("xh", hv)], inc="xfm")

    projs = [("wq", "qr", "conv"), ("wk", "kr", "conv"), ("wv", "vr", "conv"),
             ("wa", "zz", "bias"), ("wg", "gg", "sigmoid")]
    if only_proj is not None:
        projs = projs[:only_proj]
    PSH = ps("psh", [128, 512])
    nbuf = 0
    for (pi, (wn, on, mode)) in enumerate(projs):
        for fh in range(2):
            wbuf = WB[nbuf % 3]
            nbuf += 1
            in_ap = WT[wn].rearrange("(dc dp) f -> dp dc f", dp=128)[:, :, fh * 1024:(fh + 1) * 1024]
            prev_groups = max(0, (nbuf - 1) - 3 + 1) * 16
            wsem = f"wld{(nbuf - 1) % 3}"
            wv = p.dma("gpsimd", wbuf[:].rearrange("dp (dc f) -> dp dc f", dc=16), in_ap,
                       waits=[("mmdone", prev_groups * 32)], inc=wsem, tag=f"w{wn}{fh}")
            for ftl in range(8):
                f0 = fh * 1024 + ftl * 128
                ftg = fh * 8 + ftl
                for tb in range(2):
                    k = (pi * 2 + fh) * 16 + ftl * 2 + tb
                    bank = PB[k % 4]
                    do_halo = (mode == "conv" and tb == 0)

                    def mmfn(e, wbuf=wbuf, ftl=ftl, tb=tb, bank=bank, do_halo=do_halo):
                        r = None
                        for dc in range(16):
                            r = e.matmul(bank[:], wbuf[:, dc * 1024 + ftl * 128: dc * 1024 + ftl * 128 + 128],
                                         XHFM[:, dc * 1024 + tb * 512: dc * 1024 + tb * 512 + 512],
                                         start=(dc == 0), stop=(dc == 15))
                            if do_halo:
                                r = e.matmul(PSH[:, 0:16], wbuf[:, dc * 1024 + ftl * 128: dc * 1024 + ftl * 128 + 128],
                                             XHALO[:, dc * 16:(dc + 1) * 16],
                                             start=(dc == 0), stop=(dc == 15))
                        return r
                    mmw = [(wsem, wv), ("xfm", 8 * 16), ("evd", max(0, p.val("evd") - 3))]
                    if do_halo:
                        mmw += [("xfmh", 16), ("hev", p.val("hev"))]
                    mv = p.emit("tensor", mmfn, inc="mmdone", amt=32, waits=mmw, tag="mm")
                    ob = OB[k % 4]
                    obh = OBH[k % 4]
                    if mode == "conv":
                        cwc = CWs[wn[1]]
                        cbc = CBs[wn[1]]
                        if tb == 0:
                            p.emit("scalar", lambda e: e.activation(YW[:, 1:4], PSH[:, 13:16], AF.Copy),
                                   waits=[("mmdone", mv), ("cvd", p.val("cvd"))], inc="hev")
                        else:
                            p.emit("vector", lambda e: e.tensor_copy(YW[:, 0:4], YW[:, 512:516]),
                                   waits=[("evd", p.val("evd"))], inc="tlc")
                        ev = p.emit("scalar", lambda e, bank=bank: e.activation(YW[:, 4:516], bank[:], AF.Copy),
                                    waits=[("mmdone", mv), ("cvd", p.val("cvd")),
                                           ("tlc", p.val("tlc")), ("hev", p.val("hev")),
                                           ("wload", 144)], inc="evd")

                        def convfn(e, cwc=cwc, ftg=ftg):
                            e.tensor_scalar(CVA[:], YW[:, 1:513], cwc[:, ftg * 4: ftg * 4 + 1],
                                            None, ALU.mult)
                            for m in range(1, 4):
                                e.drain()
                                e.scalar_tensor_tensor(CVA[:], YW[:, 1 + m:513 + m],
                                                       cwc[:, ftg * 4 + m: ftg * 4 + m + 1],
                                                       CVA[:], ALU.mult, ALU.add)
                            return e.engine_nop()
                        cvv = p.emit("vector", convfn, waits=[("evd", ev), ("sld", max(0, p.val("sld") - 0))],
                                     inc="cvd")
                        sv = p.emit("scalar", lambda e, obh=obh, cbc=cbc, ftg=ftg: e.activation(
                            obh[:], CVA[:], AF.Silu, bias=cbc[:, ftg:ftg + 1]),
                            waits=[("cvd", cvv), ("oout", max(0, p.val("oout") - 3 * 16))], inc="sld")
                        p.dma("sync", OUT[on][f0:f0 + 128, tb * 512:(tb + 1) * 512], obh[:],
                              waits=[("sld", p.val("sld"))], inc="oout")
                        continue
                    if mode == "sigmoid":
                        def evfn(e, bank=bank, obh=obh):
                            return e.activation(obh[:], bank[:], AF.Sigmoid)
                        src = obh
                    else:
                        def evfn(e, bank=bank, ob=ob, ftl=ftl, fh=fh):
                            return e.tensor_scalar(ob[:], bank[:],
                                                   BAs[:, fh * 8 + ftl: fh * 8 + ftl + 1],
                                                   None, ALU.add)
                        src = ob
                    eng_ev = "vector" if mode == "bias" else "scalar"
                    ev = p.emit(eng_ev, evfn, waits=[("mmdone", mv), ("wload", 144),
                                                     ("oout", max(0, p.val("oout") - 3 * 16))],
                                inc="evd")
                    p.dma("sync", OUT[on][f0:f0 + 128, tb * 512:(tb + 1) * 512], src[:],
                          waits=[("evd", ev)], inc="oout")
    for tb in range(0 if skip_beta else 2):
        def mmb(e, tb=tb):
            r = None
            for dc in range(16):
                r = e.matmul(PBB[:], WBTs[:, dc * 16: dc * 16 + 16],
                             XHFM[:, dc * 1024 + tb * 512: dc * 1024 + tb * 512 + 512],
                             start=(dc == 0), stop=(dc == 15))
            return r
        mv = p.emit("tensor", mmb, waits=[("wload", 48), ("xfm", 8 * 16),
                                          ("bev", tb)], inc="bmm", tag="mmb")
        ev = p.emit("vector", lambda e, tb=tb: e.tensor_scalar(OB[3][0:16, :], PBB[:], BBs[:],
                                                               None, ALU.add),
                    waits=[("bmm", mv), ("oout", p.val("oout"))], inc="bev")
        p.dma("sync", OUT["bl"][:, tb * 512:(tb + 1) * 512], OB[3][0:16, :],
              waits=[("bev", ev)], inc="bdone")

    final = [("oout", p.val("oout"))]
    if not skip_beta:
        final.append(("bdone", p.val("bdone")))
    run_prog(nc, p, final)
    stack.close()
    return nc


def host_l1_inputs(inputs, core):
    x = np.asarray(inputs["L0_attn_x"], np.float32).reshape(8192, 2048)
    t0 = core * 1024
    xhl = np.zeros((16, 2048), np.float32)
    if t0 % 2048 != 0:
        xhl[13:16] = x[t0 - 3:t0]
    def cwl(name):
        w = np.asarray(inputs[f"L0_attn_conv_{name}_w"], np.float32)
        return np.ascontiguousarray(w[:, 0, :].reshape(16, 128, 4).transpose(1, 0, 2).reshape(128, 64))
    def cbl(name):
        b = np.asarray(inputs[f"L0_attn_conv_{name}_b"], np.float32)
        return np.ascontiguousarray(b.reshape(16, 128).T)
    d = {
        "x": np.ascontiguousarray(x[core * 1024:(core + 1) * 1024]),
        "xhl": xhl,
        "cwq": cwl("q"), "cwk": cwl("k"), "cwv": cwl("v"),
        "cbq": cbl("q"), "cbk": cbl("k"), "cbv": cbl("v"),
        "wq": np.ascontiguousarray(np.asarray(inputs["L0_attn_Wq"], np.float32).T),
        "wk": np.ascontiguousarray(np.asarray(inputs["L0_attn_Wk"], np.float32).T),
        "wv": np.ascontiguousarray(np.asarray(inputs["L0_attn_Wv"], np.float32).T),
        "wa": np.ascontiguousarray(np.asarray(inputs["L0_attn_Wa"], np.float32).T),
        "wg": np.ascontiguousarray(np.asarray(inputs["L0_attn_Wg"], np.float32).T),
        "wb": np.ascontiguousarray(np.asarray(inputs["L0_attn_Wb"], np.float32).T),
        "ba": np.ascontiguousarray(np.asarray(inputs["L0_attn_ba"], np.float32).reshape(16, 128).T),
        "bb": np.ascontiguousarray(np.asarray(inputs["L0_attn_bb"], np.float32).reshape(16, 1)),
    }
    return d



def build_l3(debug=False):
    nc = bass.Bass("TRN2", target_bir_lowering=False)
    OT = nc.dram_tensor("ot", [2048, 1024], BF16, kind="ExternalInput").ap()
    X = nc.dram_tensor("x", [1024, 2048], F32, kind="ExternalInput").ap()
    WOT = nc.dram_tensor("wot", [2048, 2048], F32, kind="ExternalInput").ap()
    AT = nc.dram_tensor("at", [2048, 512], F32, kind="ExternalInput").ap()
    BWT = nc.dram_tensor("bwt", [512, 2048], F32, kind="ExternalInput").ap()
    WGF = nc.dram_tensor("wgf", [2048, 8], F32, kind="ExternalInput").ap()
    OUTX = nc.dram_tensor("outx", [1024, 2048], F32, kind="ExternalOutput").ap()
    if debug:
        DBG1 = nc.dram_tensor("dbg1", [128, 2048], F32, kind="ExternalOutput").ap()
        DBG2 = nc.dram_tensor("dbg2", [128, 16], F32, kind="ExternalOutput").ap()
        DBG3 = nc.dram_tensor("dbg3", [128, 512], F32, kind="ExternalOutput").ap()

    p = Prog(nc)
    import contextlib
    stack = contextlib.ExitStack()
    sb = lambda name, shape, dt: stack.enter_context(nc.sbuf_tensor(name, shape, dt))
    ps = lambda name, shape: stack.enter_context(nc.psum_tensor(name, shape, F32))

    OTs = sb("ots", [128, 16 * 1024], BF16)
    WOs = sb("wos", [128, 16 * 2048], BF16)
    ATs = sb("ats", [128, 16 * 512], BF16)
    BWs = sb("bws", [128, 4 * 2048], BF16)
    WGs = sb("wgs", [128, 16 * 8], BF16)
    XR = [sb(f"xr{i}", [128, 2048], F32) for i in range(2)]
    X1 = [sb(f"x1{i}", [128, 2048], F32) for i in range(2)]
    XH2 = [sb(f"xh2{i}", [128, 2048], BF16) for i in range(2)]
    XF2 = [sb(f"xf2{i}", [128, 16 * 128], BF16) for i in range(2)]
    SQ = sb("sq3", [128, 2048], BF16)
    ST = sb("st3", [128, 8], F32)
    LG = sb("lg", [128, 16], F32)
    EX8 = sb("ex8", [128, 8], F32)
    EQ8 = sb("eq8", [128, 8], F32)
    CS8 = sb("cs8", [128, 8], F32)
    FM8 = sb("fm8", [128, 8], F32)
    Z8 = sb("z8", [128, 8], F32)
    HM = sb("hm", [128, 512], BF16)
    HMT = sb("hmt", [128, 4 * 128], BF16)
    OE = [sb(f"oe{i}", [128, 512], F32) for i in range(2)]
    DBG3_SB = sb("dbg3sb", [128, 512], F32)
    DBG1_SB = sb("dbg1sb", [128, 2048], F32)
    PA = [ps(f"pa{i}", [128, 512]) for i in range(2)]
    PLB = ps("plb", [128, 512])
    PL = PLB[:, 0:8]
    PH = ps("ph", [128, 512])
    PY = [ps(f"py{i}", [128, 512]) for i in range(2)]

    p.dma("gpsimd", OTs[:].rearrange("vp (vc t) -> vp vc t", vc=16),
          OT.rearrange("(vc vp) t -> vp vc t", vp=128), inc="wld")
    p.dma("gpsimd", WOs[:].rearrange("vp (vc d) -> vp vc d", vc=16),
          WOT.rearrange("(vc vp) d -> vp vc d", vp=128), inc="wld")
    p.dma("gpsimd", ATs[:].rearrange("dp (dc er) -> dp dc er", dc=16),
          AT.rearrange("(dc dp) er -> dp dc er", dp=128), inc="wld")
    p.dma("gpsimd", BWs[:].rearrange("ep (ec d) -> ep ec d", ec=4),
          BWT.rearrange("(ec ep) d -> ep ec d", ep=128), inc="wld")
    p.dma("gpsimd", WGs[:].rearrange("dp (dc e) -> dp dc e", dc=16),
          WGF.rearrange("(dc dp) e -> dp dc e", dp=128), inc="wld")
    p.emit("gpsimd", lambda e: e.memset(Z8[:], 0.0), inc="wld", amt=16)

    for i in range(8):
        b = i % 2
        xv = p.dma("gpsimd", XR[b][:], X[i * 128:(i + 1) * 128, :], inc="xld",
                   waits=[("rez", max(0, i - 1) * 4)])
        for db in range(4):
            bank = PA[db % 2]
            def mmo(e, i=i, db=db, bank=bank):
                r = None
                for vc in range(16):
                    r = e.matmul(bank[:], OTs[:, vc * 1024 + i * 128: vc * 1024 + (i + 1) * 128],
                                 WOs[:, vc * 2048 + db * 512:vc * 2048 + (db + 1) * 512],
                                 start=(vc == 0), stop=(vc == 15))
                return r
            mv = p.emit("tensor", mmo, waits=[("wld", 96), ("rez", max(0, p.val("rez") - 1))], inc="mmo")
            p.emit("vector", lambda e, b=b, db=db, bank=bank: e.tensor_tensor(
                X1[b][:, db * 512:(db + 1) * 512], bank[:], XR[b][:, db * 512:(db + 1) * 512], ALU.add),
                waits=[("mmo", mv), ("xld", xv), ("oev", max(0, (i - 1) * 4))], inc="rez")
        rz = p.val("rez")
        def red1(e, b=b):
            e.drain()
            return e.reduce_sum(ST[:, 0:1], X1[b][:], axis=AX)
        p.emit("vector", red1, waits=[("rez", rz)], inc="vst")
        sqv = p.emit("scalar", lambda e, b=b: e.activation(SQ[:], X1[b][:], AF.Square),
                     waits=[("rez", rz), ("sqr", i)], inc="sq")
        p.emit("vector", lambda e: e.reduce_sum(ST[:, 1:2], SQ[:], axis=AX),
               waits=[("sq", sqv)], inc="sqr")

        def stats_fn(e):
            e.drain()
            e.tensor_scalar(ST[:, 2:3], ST[:, 0:1], 1.0 / 2048, None, ALU.mult)
            e.tensor_scalar(ST[:, 3:4], ST[:, 1:2], 1.0 / 2048, None, ALU.mult)
            e.drain()
            e.tensor_tensor(ST[:, 4:5], ST[:, 2:3], ST[:, 2:3], ALU.mult)
            e.drain()
            e.tensor_tensor(ST[:, 5:6], ST[:, 3:4], ST[:, 4:5], ALU.subtract)
            e.drain()
            e.tensor_scalar(ST[:, 5:6], ST[:, 5:6], 1e-5, None, ALU.add)
            e.drain()
            return e.reciprocal(ST[:, 6:7], ST[:, 5:6])
        vv = p.emit("vector", stats_fn, inc="vst")
        rv = p.emit("scalar", lambda e: e.activation(ST[:, 7:8], ST[:, 6:7], AF.Sqrt),
                    waits=[("vst", vv), ("xh", i)], inc="rsq")

        def xh_fn(e, b=b):
            e.drain()
            return e.tensor_scalar(XH2[b][:], X1[b][:], ST[:, 2:3], ST[:, 7:8],
                                   ALU.subtract, ALU.mult)
        hv = p.emit("vector", xh_fn, waits=[("rsq", rv), ("xf2", max(0, i - 1) * 16)], inc="xh")
        out_ap = XF2[b][:].rearrange("dp (dc t) -> dp dc t", dc=16)
        p.dmat("sync", out_ap, XH2[b][:], waits=[("xh", hv)], inc="xf2")
        xfv = p.val("xf2")

        if debug and i == 0:
            def xh2cp(e, b=b):
                e.drain()
                e.tensor_copy(DBG3_SB[:, 0:8], ST[:])
                e.drain()
                return e.tensor_copy(DBG1_SB[:], XH2[b][:])
            p.emit("vector", xh2cp, waits=[("xh", 1)], inc="dbgv")
            p.dma("sync", DBG1[:], DBG1_SB[:], waits=[("dbgv", p.val("dbgv"))], inc="dbgo")
        def mml(e, b=b):
            r = None
            for dc in range(16):
                r = e.matmul(PL, XF2[b][:, dc * 128:(dc + 1) * 128], WGs[:, dc * 8:(dc + 1) * 8],
                             start=(dc == 0), stop=(dc == 15))
            return r
        lmv = p.emit("tensor", mml, waits=[("xf2", xfv), ("sel", i * 10), ("wld", 96)], inc="mml")

        def sel_fn(e):
            e.tensor_copy(LG[:, 0:8], PL)
            e.drain()
            e.reduce_max(LG[:, 8:9], LG[:, 0:8], axis=AX)
            e.drain()
            e.tensor_scalar(LG[:, 9:10], LG[:, 8:9], -1.0, None, ALU.mult)
            e.drain()
            e.tensor_scalar(EQ8[:], LG[:, 0:8], LG[:, 8:9], None, ALU.is_equal)
            e.drain()
            e.tensor_tensor_scan(CS8[:], EQ8[:], Z8[:], 0.0, ALU.add, ALU.add)
            e.drain()
            e.tensor_scalar(FM8[:], CS8[:], 1.0, None, ALU.is_le)
            e.drain()
            return e.tensor_tensor(FM8[:], FM8[:], EQ8[:], ALU.mult)
        sv = p.emit("vector", sel_fn, waits=[("mml", lmv)], inc="sel", amt=3)
        ev = p.emit("scalar", lambda e: e.activation(EX8[:], LG[:, 0:8], AF.Exp,
                                                     bias=LG[:, 9:10], scale=1.0,
                                                     accum_out=LG[:, 10:11]),
                    waits=[("sel", p.val("sel"))], inc="sel", amt=3)

        def sc_fn(e):
            e.reciprocal(LG[:, 11:12], LG[:, 10:11])
            e.drain()
            return e.tensor_scalar(LG[:, 12:13], LG[:, 11:12], 4.0, None, ALU.mult)
        scv = p.emit("vector", sc_fn, waits=[("sel", p.val("sel"))], inc="sel", amt=4)

        if debug and i == 0:
            p.dma("sync", DBG2[:], LG[:], waits=[("sel", p.val("sel"))], inc="dbgo")
            p.dma("sync", DBG3[:], DBG3_SB[:], waits=[("dbgv", p.val("dbgv"))], inc="dbgo")
        def mmh(e, b=b):
            r = None
            for dc in range(16):
                r = e.matmul(PH[:], XF2[b][:, dc * 128:(dc + 1) * 128], ATs[:, dc * 512:(dc + 1) * 512],
                             start=(dc == 0), stop=(dc == 15))
            return r
        hmv = p.emit("tensor", mmh, waits=[("xf2", xfv), ("hmd", i), ("wld", 96)], inc="mmh")
        def hv2_fn(e):
            e.drain()
            return e.scalar_tensor_tensor(
                HM[:].rearrange("p (e r) -> p e r", e=8), PH[:].rearrange("p (e r) -> p e r", e=8),
                LG[:, 12:13],
                FM8[:].rearrange("p (e one) -> p e one", one=1).broadcast_to([128, 8, 64]),
                ALU.mult, ALU.mult)
        hv2 = p.emit("vector", hv2_fn,
                     waits=[("mmh", hmv), ("sel", scv), ("hmt", i * 16)], inc="hmd")
        p.dmat("sync", HMT[:].rearrange("ep (ec t) -> ep ec t", ec=4), HM[:],
               waits=[("hmd", hv2), ("mmy", i * 4)], inc="hmt")


        for db in range(4):
            bank = PY[db % 2]
            def mmy(e, db=db, bank=bank):
                r = None
                for ec in range(4):
                    r = e.matmul(bank[:], HMT[:, ec * 128:(ec + 1) * 128],
                                 BWs[:, ec * 2048 + db * 512:ec * 2048 + (db + 1) * 512],
                                 start=(ec == 0), stop=(ec == 3))
                return r
            ymv = p.emit("tensor", mmy, waits=[("hmt", p.val("hmt")), ("oev", max(0, p.val("oev") - 1)), ("wld", 96)], inc="mmy")
            oev = p.emit("vector", lambda e, b=b, db=db, bank=bank: e.tensor_tensor(
                OE[db % 2][:], bank[:], X1[b][:, db * 512:(db + 1) * 512], ALU.add),
                waits=[("mmy", ymv), ("oz", max(0, p.val("oz") - 1 * 16))], inc="oev")
            p.dma("sync", OUTX[i * 128:(i + 1) * 128, db * 512:(db + 1) * 512], OE[db % 2][:],
                  waits=[("oev", p.val("oev"))], inc="oz")

    run_prog(nc, p, [("oz", p.val("oz"))])
    stack.close()
    return nc


def host_l3_inputs(inputs, core, o_gated):
    x = np.asarray(inputs["L0_attn_x"], np.float32).reshape(8192, 2048)
    og = np.asarray(o_gated).reshape(8192, 2048)
    A = np.asarray(inputs["L0_ffn_A"], np.float32)
    Bw = np.asarray(inputs["L0_ffn_Bw"], np.float32)
    d = {
        "ot": np.ascontiguousarray(og[core * 1024:(core + 1) * 1024].T.astype(BF)),
        "x": np.ascontiguousarray(x[core * 1024:(core + 1) * 1024]),
        "wot": np.ascontiguousarray(np.asarray(inputs["L0_attn_Wo"], np.float32).T),
        "at": np.ascontiguousarray(A.transpose(2, 0, 1).reshape(2048, 512)),
        "bwt": np.ascontiguousarray(Bw.transpose(0, 2, 1).reshape(512, 2048)),
        "wgf": np.ascontiguousarray(np.asarray(inputs["L0_ffn_Wg"], np.float32).T),
    }
    return d




CSIZE = 64
NCSCALE = 128 ** -0.5


def build_l2(nchunks=32):
    NT = CSIZE * nchunks
    NJ = NT * 8
    nc = bass.Bass("TRN2", target_bir_lowering=False)
    QF0 = nc.dram_tensor("qf", [128, NJ], BF16, kind="ExternalInput").ap()
    KF0 = nc.dram_tensor("kf", [128, NJ], BF16, kind="ExternalInput").ap()
    VF0 = nc.dram_tensor("vf", [128, NJ], BF16, kind="ExternalInput").ap()
    ZF0 = nc.dram_tensor("zf", [128, NJ], F32, kind="ExternalInput").ap()
    BL0 = nc.dram_tensor("bl", [128, nchunks * 64], F32, kind="ExternalInput").ap()
    BM0 = nc.dram_tensor("bm", [128, 16], F32, kind="ExternalInput").ap()
    GT0 = nc.dram_tensor("gt", [128, nchunks * 512], BF16, kind="ExternalInput").ap()
    NMASK0 = nc.dram_tensor("nmask", [128, 4 * 512], BF16, kind="ExternalInput").ap()
    PMASK0 = nc.dram_tensor("pmask", [128, 4 * 512], BF16, kind="ExternalInput").ap()
    SELC0 = nc.dram_tensor("selc", [128, 16 * 128], BF16, kind="ExternalInput").ap()
    SEL80 = nc.dram_tensor("sel8", [8, 128], BF16, kind="ExternalInput").ap()
    IDCB0 = nc.dram_tensor("idcb", [128, 128], BF16, kind="ExternalInput").ap()
    IDCF0 = nc.dram_tensor("idcf", [128, 128], F32, kind="ExternalInput").ap()
    LNGB0 = nc.dram_tensor("lngb", [128, 128], F32, kind="ExternalInput").ap()
    LNBB0 = nc.dram_tensor("lnbb", [128, 128], F32, kind="ExternalInput").ap()
    OO = nc.dram_tensor("oo", [128, nchunks * 512], BF16, kind="ExternalOutput").ap()

    p = Prog(nc)
    import contextlib
    stack = contextlib.ExitStack()
    sb = lambda name, shape, dt: stack.enter_context(nc.sbuf_tensor(name, shape, dt))
    ps = lambda name, shape: stack.enter_context(nc.psum_tensor(name, shape, F32))

    NB = 3
    QFC = [sb(f"qfc{i}", [128, 512], BF16) for i in range(NB)]
    KFC = [sb(f"kfc{i}", [128, 512], BF16) for i in range(NB)]
    VFC = [sb(f"vfc{i}", [128, 512], BF16) for i in range(NB)]
    ZFC = [sb(f"zfc{i}", [128, 512], F32) for i in range(NB)]
    GTC = [sb(f"gtc{i}", [128, 512], BF16) for i in range(NB)]
    BET = sb("bet", [128, nchunks * 64], F32)
    BMK = sb("bmk", [128, 16], F32)
    NMASK = sb("nmaskS", [128, 4 * 512], BF16)
    PMASK = sb("pmaskS", [128, 4 * 512], BF16)
    SELC = sb("selcS", [128, 16 * 128], BF16)
    SEL8 = sb("sel8S", [8, 128], BF16)
    IDCB = sb("idcbS", [128, 128], BF16)
    IDCF = sb("idcfS", [128, 128], F32)
    LNGB = sb("lngbS", [128, 128], F32)
    LNBB = sb("lnbbS", [128, 128], F32)
    Z64 = sb("z64", [128, 64], F32)
    SPF = sb("spf", [128, 520], F32)
    E1F = sb("e1f", [128, 512], F32)
    NM = sb("nm", [128, 8], F32)
    EMC = sb("emc", [128, 8], F32)
    NEM = sb("nem", [128, 8], F32)
    LPF = sb("lpf", [128, 512], F32)
    PSIP = sb("psip", [128, 512], F32)
    LSF = sb("lsf", [128, 512], F32)
    PHIF = sb("phif", [128, 512], F32)
    VPF = sb("vpf", [128, 512], F32)
    XCF = sb("xcf", [128, 512], F32)
    SQF = sb("sqf", [128, 512], F32)
    EET = [sb(f"eet{q}", [128, 128], F32) for q in range(4)]
    PHT = [sb(f"pht{q}", [128, 128], F32) for q in range(4)]
    GV = [[sb(f"gv{b}_{q}", [128, 512], BF16) for q in range(4)] for b in range(2)]
    PV = [[sb(f"pv{b}_{q}", [128, 512], BF16) for q in range(4)] for b in range(2)]
    UB = [sb(f"ub{q}", [128, 128], BF16) for q in range(4)]
    UBX = sb("ubx", [128, 16 * 128], BF16)
    W2V = [sb(f"w2v{q}", [128, 128], BF16) for q in range(4)]
    KTV = sb("ktv", [128, 4 * 128], BF16)
    KTB = sb("ktb", [128, 4 * 128], BF16)
    W2B = sb("w2b", [128, 4 * 128], BF16)
    PCR = sb("pcr", [8, 128], BF16)
    DIAG = sb("diag", [128, 128], BF16)
    ZA = sb("za", [128, 8 * 128], BF16)
    SBst = sb("sbst", [128, 8 * 128], BF16)
    OPR = sb("opr", [128, 128], F32)
    OSQ = sb("osq", [128, 128], BF16)
    OST = sb("ost", [128, 8], F32)
    G1T = sb("g1t", [128, 128], F32)
    G2T = sb("g2t", [128, 128], F32)
    OF = [sb(f"of{q}", [128, 128], BF16) for q in range(4)]

    PB = [ps(f"pbank{q}", [128, 128]) for q in range(4)]
    PS1 = ps("ps1", [128, 512])
    PS2 = ps("ps2", [128, 512])
    PS3 = ps("ps3", [128, 128])
    PS4 = ps("ps4", [128, 128])

    for (dst, src) in ((NMASK, NMASK0), (PMASK, PMASK0), (SELC, SELC0), (SEL8, SEL80),
                       (IDCB, IDCB0), (IDCF, IDCF0), (LNGB, LNGB0), (LNBB, LNBB0)):
        p.dma("gpsimd", dst[:], src[:], inc="cld")
    p.emit("gpsimd", lambda e: e.memset(Z64[:], 0.0), inc="cld", amt=16)
    p.emit("gpsimd", lambda e: e.memset(ZA[:], 0.0), inc="cld", amt=16)
    p.emit("gpsimd", lambda e: e.memset(SBst[:], 0.0), inc="cld", amt=16)
    p.emit("gpsimd", lambda e: e.memset(SPF[:, 0:8], 0.0), inc="cld", amt=16)
    p.emit("gpsimd", lambda e: e.memset(UBX[:], 0.0), inc="cld", amt=16)
    bv = p.dma("gpsimd", BET[:], BL0[:], inc="cld")
    p.dma("gpsimd", BMK[:], BM0[:], inc="cld")
    CLDALL = p.val("cld")
    p.emit("scalar", lambda e: e.activation(BET[:], BET[:], AF.Exp, scale=-1.0),
           waits=[("cld", CLDALL)], inc="bsig")
    nrep = (nchunks * 64) // 16
    def bfin(e):
        e.drain()
        e.tensor_scalar(BET[:], BET[:], 1.0, None, ALU.add)
        e.drain()
        e.reciprocal(BET[:], BET[:])
        e.drain()
        return e.tensor_tensor(
            BET[:], BET[:],
            BMK[:].rearrange("p (one m) -> p one m", one=1).broadcast_to([128, nrep, 16]),
            ALU.mult)
    p.emit("vector", bfin, waits=[("bsig", 1)], inc="bsig")

    for c in range(nchunks):
        b = c % NB
        b2 = c % 2
        J0 = c * 512
        wdep = [("ochunk", max(0, c - (NB - 1)))]
        for dst, src in ((QFC[b], QF0), (KFC[b], KF0), (VFC[b], VF0)):
            p.dma("gpsimd", dst[:], src[:, J0:J0 + 512], waits=wdep, inc=f"ld{b}")
        p.dma("gpsimd", ZFC[b][:], ZF0[:, J0:J0 + 512], waits=wdep, inc=f"ld{b}")
        p.dma("gpsimd", GTC[b][:], GT0[:, J0:J0 + 512], waits=wdep, inc=f"ld{b}")
        LD = p.val(f"ld{b}")

        p.emit("scalar", lambda e, b=b: e.activation(E1F[:], ZFC[b][:], AF.Exp, scale=-1.0),
               waits=[(f"ld{b}", LD), ("cld", CLDALL), ("dk", p.val("dk")), ("fin", c * 64),
                      ("spf", p.val("spf"))], inc="e1f")
        sp1 = p.emit("scalar", lambda e: e.activation(SPF[:, 8:520], E1F[:], AF.Ln, bias=1.0),
                     waits=[("e1f", p.val("e1f"))], inc="spf")
        def nm_fn(e):
            r = None
            for bh in range(8):
                r = e.reduce_sum(NM[:, bh:bh + 1], SPF[:, 8 + bh: 8 + bh + 32 * 8: 8],
                                 axis=AX, negate=True)
            return r
        nmv = p.emit("vector", nm_fn, waits=[("spf", p.val("spf")), ("dke", p.val("dke"))], inc="dk")
        emv = p.emit("scalar", lambda e: e.activation(EMC[:], NM[:], AF.Exp),
                     waits=[("dk", nmv), ("dgb", p.val("dgb")), ("ps2e", p.val("ps2e"))], inc="dke")
        nemv = p.emit("vector", lambda e: e.tensor_scalar(NEM[:], EMC[:], -NCSCALE, None, ALU.mult),
                      waits=[("dke", emv)], inc="dk")
        def scan_fn(e):
            e.drain()
            r = None
            for bh in range(8):
                r = e.tensor_tensor_scan(LPF[:, bh:bh + 505:8], SPF[:, bh:bh + 505:8],
                                         Z64[:], NM[:, bh:bh + 1], ALU.add, ALU.add)
            return r
        scv = p.emit("vector", scan_fn, waits=[("dk", nmv), ("dke", p.val("dke"))], inc="dk")
        psv = p.emit("scalar", lambda e: e.activation(PSIP[:], LPF[:], AF.Exp),
                     waits=[("dk", scv)], inc="dke")
        lsv = p.emit("vector", lambda e: e.tensor_tensor(LSF[:], LPF[:], SPF[:, 8:520], ALU.add),
                     waits=[("dk", scv), ("dke", p.val("dke"))], inc="dk")
        phv = p.emit("scalar", lambda e: e.activation(PHIF[:], LSF[:], AF.Exp, scale=-1.0),
                     waits=[("dk", lsv), ("ps1m", p.val("ps1m")), ("dgb", p.val("dgb")),
                            ("ps3m", p.val("ps3m"))], inc="dke")
        vpv = p.emit("vector", lambda e, b=b: e.tensor_tensor(VPF[:], PSIP[:], VFC[b][:], ALU.mult),
                     waits=[("dke", psv), (f"ld{b}", LD)], inc="dk")

        for q in range(4):
            tv = p.emit("tensor", lambda e, b=b, q=q: e.transpose(
                PS1[:, 0:128], ZFC[b][:, q * 128:(q + 1) * 128], IDCF[:]),
                waits=[(f"ld{b}", LD), ("cld", CLDALL), ("ps1e", p.val("ps1e"))], inc="ps1m")
            p.emit("scalar", lambda e, q=q: e.activation(EET[q][:], PS1[:, 0:128], AF.Exp, scale=-1.0),
                   waits=[("ps1m", tv), ("fin", c * 64)], inc="ps1e")
        def eet1(e):
            r = None
            for q in range(4):
                r = e.tensor_scalar(EET[q][:], EET[q][:], 1.0, None, ALU.add)
            return r
        eev = p.emit("vector", eet1, waits=[("ps1e", p.val("ps1e"))], inc="eet1")

        for q in range(4):
            gm = p.emit("tensor", lambda e, b=b, q=q: e.matmul(
                PS1[:], KFC[b][:, q * 128:(q + 1) * 128], KFC[b][:], start=True, stop=True),
                waits=[(f"ld{b}", LD), ("ps1e", p.val("ps1e"))], inc="ps1m")
            p.emit("vector", lambda e, b2=b2, b=b, q=q: e.tensor_tensor(
                GV[b2][q][:], PS1[:], NMASK[:, q * 512:(q + 1) * 512], ALU.mult),
                waits=[("ps1m", gm), ("fin", max(0, c - 1) * 64)], inc="ps1e")
            pm = p.emit("tensor", lambda e, b=b, q=q: e.matmul(
                PS1[:], KFC[b][:, q * 128:(q + 1) * 128], QFC[b][:], start=True, stop=True),
                waits=[("ps1e", p.val("ps1e"))], inc="ps1m")
            p.emit("vector", lambda e, b2=b2, q=q: e.tensor_tensor(
                PV[b2][q][:], PS1[:], PMASK[:, q * 512:(q + 1) * 512], ALU.mult),
                waits=[("ps1m", pm), ("ogth", max(0, c - 1) * 10)], inc="ps1e")

        for bh in range(8):
            p.emit("tensor", lambda e, b=b, bh=bh: e.matmul(
                PS2[:, bh * 64:(bh + 1) * 64], ZA[:, bh * 128:(bh + 1) * 128],
                KFC[b][:, bh:bh + 505:8], start=True, stop=True),
                waits=[(f"ld{b}", LD), ("ps2e", p.val("ps2e")), ("sae", c * 16)], inc="ps2m")
        w0v = p.val("ps2m")
        def xc_fn(e):
            e.drain()
            r = None
            for bh in range(8):
                r = e.scalar_tensor_tensor(
                    XCF[:, bh:bh + 505:8], PS2[:, bh * 64:(bh + 1) * 64], NEM[:, bh:bh + 1],
                    VPF[:, bh:bh + 505:8], ALU.mult, ALU.add)
            return r
        xcv = p.emit("vector", xc_fn, waits=[("ps2m", w0v), ("dk", vpv),
                                             ("pinit", c * 4)], inc="ps2e")
        xcv2 = p.emit("vector", lambda e: e.engine_nop(), inc="xcv")

        for bh in range(8):
            p.emit("tensor", lambda e, b=b, bh=bh: e.matmul(
                PS2[:, bh * 64:(bh + 1) * 64], ZA[:, bh * 128:(bh + 1) * 128],
                QFC[b][:, bh:bh + 505:8], start=True, stop=True),
                waits=[("ps2e", p.val("ps2e"))], inc="ps2m")
        sqmv = p.val("ps2m")
        def sq_fn(e):
            r = None
            for bh in range(8):
                r = e.tensor_scalar(SQF[:, bh:bh + 505:8], PS2[:, bh * 64:(bh + 1) * 64],
                                    EMC[:, bh:bh + 1], None, ALU.mult)
            return r
        sqev = p.emit("vector", sq_fn, waits=[("ps2m", sqmv), ("dke", emv),
                                              ("oinit", c * 4)], inc="ps2e")

        for q in range(4):
            itv = p.emit("tensor", lambda e, q=q: e.transpose(
                PB[q][:], XCF[:, q * 128:(q + 1) * 128], IDCF[:]),
                waits=[("xcv", xcv2), ("fin", c * 64), ("oevq", c * 4)], inc="pinit")
            p.emit("vector", lambda e, q=q: e.memset(UB[q][:], 0.0),
                   waits=[("w2v", max(0, c - 1) * 4), ("ogth", max(0, c - 1) * 10),
                          ("gth", p.val("gth"))], inc="ucl")
            for q2 in range(q):
                p.emit("tensor", lambda e, b2=b2, q=q, q2=q2: e.matmul(
                    PB[q][:], GV[b2][q2][:, q * 128:(q + 1) * 128], UB[q2][:],
                    start=False, stop=False),
                    waits=[("ucp", c * 64 + (q2 + 1) * 16)], inc="gth")
            for ts in range(16):
                a64 = (ts // 8) * 64
                tglob = c * 64 + q * 16 + ts
                fv = p.emit("vector", lambda e, q=q, ts=ts, a64=a64, tglob=tglob: e.scalar_tensor_tensor(
                    UBX[a64:a64 + 64, ts * 128:(ts + 1) * 128],
                    PB[q][a64:a64 + 64, :],
                    BET[a64:a64 + 64, tglob: tglob + 1],
                    EET[q][a64:a64 + 64, :], ALU.mult, ALU.mult),
                    waits=[("pinit", p.val("pinit")), ("gth", p.val("gth")),
                           ("sct", p.val("sct")), ("eet1", eev), ("bsig", 2),
                           ("w2v", max(0, c - 1) * 4), ("ogth", max(0, c - 1) * 10),
                           ("ucp", max(0, c * 64 + (q - 3) * 16))],
                    inc="fin")
                if ts >= 1:
                    pa = ((ts - 1) // 8) * 64
                    p.emit("vector", lambda e, q=q, ts=ts, pa=pa: e.tensor_tensor(
                        UB[q][pa:pa + 64, :], UB[q][pa:pa + 64, :],
                        UBX[pa:pa + 64, (ts - 1) * 128:ts * 128], ALU.add), inc="ucp")
                if ts < 15:
                    p.emit("tensor", lambda e, b2=b2, q=q, ts=ts: e.matmul(
                        PB[q][:], GV[b2][q][:, q * 128:(q + 1) * 128],
                        UBX[:, ts * 128:(ts + 1) * 128], start=False, stop=False),
                        waits=[("fin", p.val("fin"))], inc="sct")
            def lastcp(e, q=q):
                e.drain()
                return e.tensor_tensor(UB[q][64:128, :], UB[q][64:128, :],
                                       UBX[64:128, 15 * 128:16 * 128], ALU.add)
            p.emit("vector", lastcp, inc="ucp")
        FIN = p.val("fin")

        ktv = p.emit("sync", (lambda e, b=b: e.dma_start(
            KTV[:].rearrange("p (q d) -> p q d", q=4), KFC[b][:], transpose=True)),
            waits=[(f"ld{b}", LD), ("selk", c * 16)], inc="kttr", amt=16, tag="dma")

        ptv = p.emit("tensor", lambda e: e.transpose(PS3[0:8, :], PHIF[:, 504:512], IDCF[:]),
                     waits=[("dke", phv), ("ps3e", p.val("ps3e"))], inc="ps3m")
        pev = p.emit("scalar", lambda e: e.activation(PCR[:], PS3[0:8, :], AF.Copy),
                     waits=[("ps3m", ptv), ("bcmm", p.val("bcmm"))], inc="ps3e")
        bcv = p.emit("tensor", lambda e: e.matmul(PS4[:], SEL8[:], PCR[:], start=True, stop=True),
                     waits=[("ps3e", pev), ("ps4e", p.val("ps4e"))], inc="bcmm")
        def w2_fn(e):
            e.drain()
            r = None
            for q in range(4):
                r = e.tensor_tensor(W2V[q][:], UB[q][:], PS4[:], ALU.mult)
            return r
        w2vv = p.emit("vector", w2_fn, waits=[("bcmm", bcv), ("ucp", (c + 1) * 64),
                                              ("selw", c * 16)], inc="w2v", amt=4)

        for (dst, src_tiles, sem_mm, sem_ev, wname, wval) in (
                (KTB, [KTV[:, qj * 128:(qj + 1) * 128] for qj in range(4)], "selk", "evk", "kttr", p.val("kttr")),
                (W2B, [W2V[qj][:] for qj in range(4)], "selw", "evw", "w2v", p.val("w2v"))):
            for q2 in range(4):
                mmv = None
                def selmm(e, src_tiles=src_tiles, q2=q2):
                    r = None
                    for qj in range(4):
                        r = e.matmul(PS2[:, 0:128], SELC[:, (qj * 4 + q2) * 128:(qj * 4 + q2 + 1) * 128],
                                     src_tiles[qj], start=(qj == 0), stop=(qj == 3))
                    return r
                mmv = p.emit("tensor", selmm, waits=[(wname, wval), ("ps2e", p.val("ps2e"))],
                             inc="ps2m")
                p.emit("scalar", lambda e, dst=dst, q2=q2: e.activation(
                    dst[:, q2 * 128:(q2 + 1) * 128], PS2[:, 0:128], AF.Copy),
                    waits=[("ps2m", mmv), ("sav", c * 32)], inc="ps2e")
                p.emit("scalar", lambda e: e.nop(), inc=sem_mm, amt=4)
        EVK = p.val("ps2e")

        for bh in range(8):
            dgv = p.emit("vector", lambda e, bh=bh: e.tensor_scalar(
                DIAG[:], IDCB[:], PHIF[:, 504 + bh:505 + bh], EMC[:, bh:bh + 1], ALU.mult, ALU.mult),
                waits=[("dke", phv), ("sav", c * 32 + bh * 4)], inc="dgb")
            def sa_fn(e, bh=bh):
                half = (bh % 2) * 64
                pair = bh // 2
                kt = KTB[half:half + 64, pair * 128:(pair + 1) * 128]
                w2 = W2B[half:half + 64, pair * 128:(pair + 1) * 128]
                e.matmul(PS3[:], kt, w2, start=True, stop=False)
                e.matmul(PS3[:], SBst[:, bh * 128:(bh + 1) * 128], DIAG[:], start=False, stop=True)
                e.matmul(PS4[:], w2, kt, start=True, stop=False)
                return e.matmul(PS4[:], DIAG[:], SBst[:, bh * 128:(bh + 1) * 128], start=False, stop=True)
            sav = p.emit("tensor", sa_fn, waits=[("ps2e", EVK), ("dgb", p.val("dgb")),
                                                 ("ps3e", p.val("ps3e")), ("ps4e", p.val("ps4e")),
                                                 ("sae", c * 16 + bh * 2)],
                         inc="sav", amt=4)
            p.emit("scalar", lambda e, bh=bh: e.activation(ZA[:, bh * 128:(bh + 1) * 128], PS3[:], AF.Copy),
                   waits=[("sav", p.val("sav"))], inc="sae")
            p.emit("scalar", lambda e, bh=bh: e.activation(SBst[:, bh * 128:(bh + 1) * 128], PS4[:], AF.Copy),
                   waits=[("sav", p.val("sav"))], inc="sae")
            p.emit("scalar", lambda e: e.nop(), inc="ps3e")
            p.emit("scalar", lambda e: e.nop(), inc="ps4e")

        for q in range(4):
            p.emit("tensor", lambda e, q=q: e.transpose(PB[q][:], SQF[:, q * 128:(q + 1) * 128], IDCF[:]),
                   waits=[("ps2e", sqev), ("fin", FIN), ("oevq", c * 4)], inc="oinit")
        OINIT = p.val("oinit")
        for q in range(4):
            for q2 in range(q + 1):
                p.emit("tensor", lambda e, b2=b2, q=q, q2=q2: e.matmul(
                    PB[q][:], PV[b2][q2][:, q * 128:(q + 1) * 128], UB[q2][:],
                    start=False, stop=False),
                    waits=[("oinit", OINIT), ("ucp", (c + 1) * 64)], inc="ogth")
        OGT = p.val("ogth")
        for q in range(4):
            tv = p.emit("tensor", lambda e, q=q: e.transpose(PS1[:, 0:128], PHIF[:, q * 128:(q + 1) * 128], IDCF[:]),
                        waits=[("dke", phv), ("ps1e", p.val("ps1e"))], inc="ps1m")
            p.emit("scalar", lambda e, q=q: e.activation(PHT[q][:], PS1[:, 0:128], AF.Copy),
                   waits=[("ps1m", tv), ("oevq", c * 4)], inc="ps1e")
        PHTD = p.val("ps1e")

        for q in range(4):
            def oev_fn(e, q=q):
                e.drain()
                e.tensor_tensor(OPR[:], PB[q][:], PHT[q][:], ALU.mult)
                e.drain()
                return e.reduce_sum(OST[:, 0:1], OPR[:], axis=AX)
            ov = p.emit("vector", oev_fn, waits=[("ogth", OGT), ("ps1e", PHTD),
                                                 ("odone", c * 4 + q)], inc="oevq")
            sv2 = p.emit("vector", lambda e: e.tensor_tensor(OSQ[:], OPR[:], OPR[:], ALU.mult),
                         waits=[("oevq", ov)], inc="osq")
            def ostat(e):
                e.drain()
                e.reduce_sum(OST[:, 1:2], OSQ[:], axis=AX)
                e.drain()
                e.tensor_scalar(OST[:, 2:3], OST[:, 0:1], 1.0 / 128, None, ALU.mult)
                e.tensor_scalar(OST[:, 3:4], OST[:, 1:2], 1.0 / 128, None, ALU.mult)
                e.drain()
                e.tensor_tensor(OST[:, 4:5], OST[:, 2:3], OST[:, 2:3], ALU.mult)
                e.drain()
                e.tensor_tensor(OST[:, 5:6], OST[:, 3:4], OST[:, 4:5], ALU.subtract)
                e.drain()
                return e.tensor_scalar(OST[:, 5:6], OST[:, 5:6], 1e-5, None, ALU.add)
            ov2 = p.emit("vector", ostat, waits=[("osq", sv2)], inc="oev2")
            rv0 = p.emit("scalar", lambda e: e.activation(OST[:, 6:7], OST[:, 5:6], AF.Ln),
                         waits=[("oev2", ov2)], inc="orsq")
            rv = p.emit("scalar", lambda e: e.activation(OST[:, 7:8], OST[:, 6:7], AF.Exp, scale=-0.5),
                        waits=[("orsq", rv0)], inc="orsq")
            def ofin(e, q=q, b=b):
                e.drain()
                e.tensor_scalar(OPR[:], OPR[:], OST[:, 2:3], OST[:, 7:8], ALU.subtract, ALU.mult)
                e.tensor_tensor(G1T[:], LNGB[:], GTC[b][:, q * 128:(q + 1) * 128], ALU.mult)
                e.tensor_tensor(G2T[:], LNBB[:], GTC[b][:, q * 128:(q + 1) * 128], ALU.mult)
                e.drain()
                e.tensor_tensor(OPR[:], OPR[:], G1T[:], ALU.mult)
                e.drain()
                return e.tensor_tensor(OF[q][:], OPR[:], G2T[:], ALU.add)
            ov3 = p.emit("vector", ofin, waits=[("orsq", rv), ("oout", c * 4 * 16)], inc="odone")
            p.dma("sync", OO[:, J0 + q * 128: J0 + (q + 1) * 128], OF[q][:],
                  waits=[("odone", p.val("odone"))], inc="oout")
        p.emit("sync", lambda e: e.nop(), waits=[("oout", p.val("oout")), ("sae", (c + 1) * 16)],
               inc="ochunk")

    run_prog(nc, p, [("oout", p.val("oout")), ("ochunk", p.val("ochunk"))])
    stack.close()
    return nc


C2 = 1.0 / 128.0
C1 = 128 ** -0.5


def l2_consts():
    j = np.arange(512)
    t = j >> 3
    bh = j & 7
    nmask = np.zeros((512, 512), np.float32)
    pmask = np.zeros((512, 512), np.float32)
    same = bh[:, None] == bh[None, :]
    nmask[same & (t[:, None] < t[None, :])] = -C2
    pmask[same & (t[:, None] <= t[None, :])] = C1
    NM = np.zeros((128, 4 * 512), BF)
    PM = np.zeros((128, 4 * 512), BF)
    for qr in range(4):
        NM[:, qr * 512:(qr + 1) * 512] = nmask[qr * 128:(qr + 1) * 128, :].astype(BF)
        PM[:, qr * 512:(qr + 1) * 512] = pmask[qr * 128:(qr + 1) * 128, :].astype(BF)
    SELC = np.zeros((128, 16 * 128), BF)
    for qj in range(4):
        for q2 in range(4):
            tile = np.zeros((128, 128), np.float32)
            for jl in range(128):
                jg = qj * 128 + jl
                j2g = (jg & 7) * 64 + (jg >> 3)
                if q2 * 128 <= j2g < (q2 + 1) * 128:
                    tile[jl, j2g - q2 * 128] = 1.0
            SELC[:, (qj * 4 + q2) * 128:(qj * 4 + q2 + 1) * 128] = tile.astype(BF)
    SEL8 = np.zeros((8, 128), np.float32)
    for b in range(8):
        SEL8[b, np.arange(b, 128, 8)] = C1
    BM = np.zeros((128, 16), np.float32)
    for m in range(16):
        BM[m * 8:(m + 1) * 8, m] = 1.0
    return dict(nmask=np.ascontiguousarray(NM), pmask=np.ascontiguousarray(PM),
                selc=np.ascontiguousarray(SELC), sel8=np.ascontiguousarray(SEL8.astype(BF)),
                idcb=np.ascontiguousarray(np.eye(128, dtype=np.float32).astype(BF)),
                idcf=np.ascontiguousarray(np.eye(128, dtype=np.float32)),
                bm=np.ascontiguousarray(BM))


_L2C = l2_consts()
_NC1 = None
_NC2 = None
_NC3 = None


def kernel(**inputs):
    global _NC1, _NC2, _NC3
    if _NC1 is None:
        _NC1 = build_l1()
        _NC2 = build_l2(nchunks=32)
        _NC3 = build_l3()
    in1 = [host_l1_inputs(inputs, c) for c in range(8)]
    r1 = run_bass_kernel_spmd(_NC1, in1, core_ids=list(range(8)))
    cat = lambda n: np.concatenate([np.asarray(r1.results[c][n]) for c in range(8)], axis=1)
    qc, kc, vc, gg = cat("qr"), cat("kr"), cat("vr"), cat("gg")
    zz, bl = cat("zz"), cat("bl")

    ln_g = np.asarray(inputs["L0_attn_ln_g"], np.float32)
    ln_b = np.asarray(inputs["L0_attn_ln_b"], np.float32)
    lngb = np.ascontiguousarray(np.broadcast_to(ln_g, (128, 128)).astype(np.float32).copy())
    lnbb = np.ascontiguousarray(np.broadcast_to(ln_b, (128, 128)).astype(np.float32).copy())
    tsel = (np.arange(2048) % 16)
    in2 = []
    for core in range(8):
        bp, hq = core // 4, core % 4
        qf = np.empty((128, 16384), BF)
        kf = np.empty((128, 16384), BF)
        vf = np.empty((128, 16384), BF)
        zf = np.empty((128, 16384), np.float32)
        BL2 = np.zeros((128, 2048), np.float32)
        GT = np.empty((128, 16384), BF)
        for b in range(2):
            for h in range(4):
                bh = b * 4 + h
                hb = (4 * hq + h) * 128
                toks = slice((2 * bp + b) * 2048, (2 * bp + b) * 2048 + 2048)
                qf[:, bh::8] = qc[hb:hb + 128, toks]
                kf[:, bh::8] = kc[hb:hb + 128, toks]
                vf[:, bh::8] = vc[hb:hb + 128, toks]
                zf[:, bh::8] = zz[hb:hb + 128, toks]
                BL2[tsel * 8 + bh, np.arange(2048)] = bl[4 * hq + h, toks]
                gslab = gg[hb:hb + 128, toks]
                for ts in range(16):
                    GT[ts * 8 + bh] = gslab[:, ts::16].T.reshape(-1)
        in2.append(dict(qf=np.ascontiguousarray(qf), kf=np.ascontiguousarray(kf),
                        vf=np.ascontiguousarray(vf), zf=np.ascontiguousarray(zf),
                        bl=np.ascontiguousarray(BL2), gt=np.ascontiguousarray(GT),
                        lngb=lngb, lnbb=lnbb, **_L2C))
    r2 = run_bass_kernel_spmd(_NC2, in2, core_ids=list(range(8)))

    og = np.empty((8192, 2048), BF)
    for core in range(8):
        bp, hq = core // 4, core % 4
        OO = np.asarray(r2.results[core]["oo"])
        for b in range(2):
            for h in range(4):
                bh = b * 4 + h
                hb = (4 * hq + h) * 128
                t_off = (2 * bp + b) * 2048
                for ts in range(16):
                    og[t_off + ts:t_off + 2048:16, hb:hb + 128] = OO[ts * 8 + bh].reshape(128, 128)

    in3 = [host_l3_inputs(inputs, c, og) for c in range(8)]
    r3 = run_bass_kernel_spmd(_NC3, in3, core_ids=list(range(8)))
    out = np.concatenate([np.asarray(r3.results[c]["outx"]) for c in range(8)], axis=0)
    return np.ascontiguousarray(out.reshape(4, 2048, 2048).astype(np.float32))
